# version 05
# baseline (speedup 1.0000x reference)
import numpy as np
import ml_dtypes
import concourse.bass as bass
import concourse.mybir as mybir
from concourse.bass_utils import run_bass_kernel_spmd

F32 = mybir.dt.float32
BF16 = mybir.dt.bfloat16
AF = mybir.ActivationFunctionType
ALU = mybir.AluOpType
BF = ml_dtypes.bfloat16

C = 1024
NH = 16
HS = 64
NJC = C // 128
L = 64
E = 8
HF = 2048
NKH = HF // 128
GN_EPS = 6.4e-4
LN_EPS = 1e-5


class P:
    ENGS = ("t", "v", "s", "g", "y")

    def __init__(self, nc):
        self.nc = nc
        self.q = {e: [] for e in P.ENGS}
        self._sems = {}
        self._cnt = {}
        self._lastw = {}

    def sem(self, name):
        h = self.nc.semaphore(name).__enter__()
        self._sems[name] = h
        self._cnt[name] = 0
        return name

    def n(self, s):
        return self._cnt[s]

    def wait(self, eng, s, val=None):
        v = self._cnt[s] if val is None else val
        assert v <= self._cnt[s], f"wait on future value of {s}: {v} > {self._cnt[s]}"
        h = self._sems[s]
        key = (eng, s)
        if v > 0 and v > self._lastw.get(key, 0):
            self._lastw[key] = v
            self.q[eng].append(lambda e, h=h, v=v: e.wait_ge(h, v))

    t_rg = 0

    def trun(self, rg, fn, inc=None, k=1):
        if rg != self.t_rg:
            self.q["t"].append(lambda e: e.nop())
            self.t_rg = rg
        self.run("t", fn, inc=inc, k=k)

    def run(self, eng, fn, inc=None, k=1):
        if inc is None:
            self.q[eng].append(fn)
        else:
            h = self._sems[inc]
            def wrapped(e, fn=fn, h=h, k=k):
                r = fn(e)
                r.then_inc(h, k)
            self.q[eng].append(wrapped)
            self._cnt[inc] += k

    def emit(self):
        nc = self.nc
        blk = nc.Block().__enter__()
        def runner(qname):
            def go(eng):
                for f in self.q[qname]:
                    f(eng)
            return go
        blk.tensor(runner("t"))
        blk.vector(runner("v"))
        blk.scalar(runner("s"))
        blk.gpsimd(runner("g"))
        blk.sync(runner("y"))
        blk.__exit__(None, None, None)


def build_program(T=4096, TT=512, DBG=False, HOIST=True):
    NT = T // TT
    NTC = TT // 128
    NCH = TT // L

    nc = bass.Bass("TRN2", target_bir_lowering=False, detect_race_conditions=False)
    dram = {}
    def din(name, shape, dt=F32):
        dram[name] = nc.dram_tensor(name, list(shape), dt, kind="ExternalInput").ap()
        return dram[name]

    x_h = din("x", (T, C))
    w_r = din("w_r", (C, C)); w_k = din("w_k", (C, C)); w_v = din("w_v", (C, C))
    w_g = din("w_g", (C, C)); w_o = din("w_o", (C, C))
    w1_h = din("ffn_w1", (E, C, HF)); w2_h = din("ffn_w2", (E, HF, C))
    maa_w1_h = din("maa_w1", (C, 160))
    w2aug_h = din("w2aug", (33, 5, C))
    dw1_h = din("dw1", (C, HS))
    dw2aug_h = din("dw2aug", (65, C))
    rw_h = din("rw", (C, E))
    b1f_h = din("b1f", (128, E * NKH))
    b2rep_h = din("b2rep", (128, C))
    rbrep_h = din("rbrep", (128, E))
    ufeat_h = din("ufeat", (128, NJC))
    maax_h = din("maax_pp", (128, NJC))
    lnw_h = din("lnw_pp", (128, NJC)); lnb_h = din("lnb_pp", (128, NJC))
    fg_h = din("ffng_pp", (128, NJC)); fb_h = din("ffnb_pp", (128, NJC))
    ident_h = din("c_ident", (128, 128), BF16)
    negtri_h = din("c_negtri", (128, 128), BF16)
    msk_ms_h = din("c_ms", (128, NH * L), BF16)
    msk_id_h = din("c_id", (128, NH * L), BF16)
    eh_h = din("c_eh", (128, 2), BF16)
    e2t_h = din("c_e2t", (2, 128), BF16)

    out_h = nc.dram_tensor("out", [T, C], F32, kind="ExternalOutput").ap()
    dbg = {}
    if DBG:
        for nm, shape in [("d_xs", (C, TT)), ("d_r", (C, TT)), ("d_k", (C, TT)),
                          ("d_v", (TT, C)), ("d_ew", (TT, C)), ("d_y", (C, TT)),
                          ("d_yng", (C, TT)), ("d_x1", (TT, C)), ("d_z", (C, TT)),
                          ("d_wgt", (TT, E)), ("d_mixed", (TT, C)),
                          ("d_x1z", (TT, C))]:
            dbg[nm] = nc.dram_tensor(nm, list(shape), F32, kind="ExternalOutput").ap()

    s_wr = nc.dram_tensor("s_wr", [C, C], BF16, kind="Internal").ap()
    s_wk = nc.dram_tensor("s_wk", [C, C], BF16, kind="Internal").ap()
    s_wv = nc.dram_tensor("s_wv", [C, C], BF16, kind="Internal").ap()
    s_wg = nc.dram_tensor("s_wg", [C, C], BF16, kind="Internal").ap()
    s_wo = nc.dram_tensor("s_wo", [C, C], BF16, kind="Internal").ap()
    s_w1 = nc.dram_tensor("s_w1", [E, C, HF], BF16, kind="Internal").ap()
    s_w2 = nc.dram_tensor("s_w2", [E, HF, C], BF16, kind="Internal").ap()

    import contextlib
    es = contextlib.ExitStack()
    def sb(name, shape, dt):
        return es.enter_context(nc.sbuf_tensor(name, list(shape), dt))

    x_tok = sb("x_tok", (128, NTC * C), F32)
    xfeat = sb("xfeat", (128, NJC * (TT + 1)), BF16)
    rg_xx = sb("rg_xx", (128, TT * NJC), BF16)
    rg_xs = sb("rg_xs", (128, TT * NJC), BF16)
    rg_rk = sb("rg_rk", (128, NTC * C), F32)
    rg_vew = sb("rg_vew", (128, 2 * NTC * C), BF16)
    g_sb = sb("g_sb", (128, NJC * TT), BF16)
    ybf = sb("ybf", (128, NJC * TT), BF16)
    x1_tok = sb("x1_tok", (128, NTC * C), F32)
    e1b = sb("e1b", (128, C // 2), BF16)
    e2b = sb("e2b", (128, C // 2), BF16)
    rt_b = sb("rt_b", (128, C // 2), BF16)
    kt_b = sb("kt_b", (128, C // 2), BF16)
    ru_b = sb("ru_b", (128, C // 2), BF16)
    ktok = sb("ktok", (128, NJC * 128), BF16)
    a_sb = sb("a_sb", (128, NH * L), BF16)
    a2m = sb("a2m", (128, NH * L), BF16)
    s_st = sb("s_st", (128, NJC * HS), F32)
    s_bf = sb("s_bf", (128, NJC * HS), BF16)
    d_b = sb("d_b", (128, NJC), F32)
    tanhb = sb("tanhb", (65, TT), BF16)
    mixb = sb("mixb", (33, 5 * TT), BF16)
    gab = sb("gab", (2, 4 * TT), BF16)
    lg_b = sb("lg_b", (128, NTC * E), F32)
    r_m1 = sb("r_m1", (128, NTC), F32)
    r_m2 = sb("r_m2", (128, NTC), F32)
    r_t1 = sb("r_t1", (128, NTC * E), F32)
    wgt_pad = sb("wgt_pad", (128, 128), F32)
    wgt_bf = sb("wgt_bf", (128, 128), BF16)
    wgt_t = sb("wgt_t", (32, 512), BF16)
    wa = sb("wa", (128, 2 * NJC * 512), BF16)
    wb = sb("wb", (128, NJC * C), BF16)
    wc = sb("wc", (128, NJC * C), BF16)
    maa1 = sb("maa1", (128, NJC * 160), BF16)
    w2au = sb("w2au", (33, 5 * C), BF16)
    dw1b = sb("dw1b", (128, NJC * HS), BF16)
    dw2b = sb("dw2b", (65, C), BF16)
    rwb = sb("rwb", (128, NJC * E), BF16)
    b1f = sb("b1f_sb", (128, E * NKH), F32)
    b2b = sb("b2b", (128, C), BF16)
    rbb = sb("rbb", (128, E), F32)
    uft = sb("uft", (128, NJC), F32)
    maax = sb("maax", (128, NJC), F32)
    lnw = sb("lnw", (128, NJC), F32)
    lnb = sb("lnb", (128, NJC), F32)
    ffg = sb("ffg", (128, NJC), F32)
    ffb = sb("ffb", (128, NJC), F32)
    c_id = sb("c_idn", (128, 128), BF16)
    c_tri = sb("c_tri", (128, 128), BF16)
    c_ms = sb("c_msb", (128, NH * L), BF16)
    c_dg = sb("c_dgb", (128, NH * L), BF16)
    c_eh = sb("c_ehb", (128, 2), BF16)
    c_e2t = sb("c_e2tb", (2, 128), BF16)
    r_g1 = sb("r_g1", (2, TT), F32)
    r_g2 = sb("r_g2", (2, TT), F32)
    r_g3 = sb("r_g3", (2, TT), F32)
    ln_a = sb("ln_a", (128, NTC), F32)
    ln_b2 = sb("ln_b2", (128, NTC), F32)
    ln_c = sb("ln_c", (128, NTC), F32)
    ln_d = sb("ln_d", (128, NTC), F32)

    xx_v = rg_xx[:, 0:NJC * TT]
    xbfs_v = rg_xx
    x1z_v = rg_xx[:, 0:NTC * C]
    xs_v = rg_xs[:, 0:NJC * TT]
    y2_v = rg_xs[:, 0:NJC * TT]
    sq_v = rg_xs[:, 0:2048].bitcast(F32)
    sq2_v = rg_xs[:, 2048:4096].bitcast(F32)
    rk_bf = rg_rk[:].bitcast(BF16)
    r_v = rk_bf[:, 0:NJC * TT]
    k_v = rk_bf[:, NJC * TT:2 * NJC * TT]
    mixed_v = rg_rk
    v_v = rg_vew[:, 0:NTC * C]
    ew_v = rg_vew[:, NTC * C:2 * NTC * C]
    h_v = rg_vew
    z_v = ybf

    def ps(name, shape, dt):
        return es.enter_context(nc.psum_tensor(name, list(shape), dt))
    P0 = ps("P0", (128, 512), F32)
    P1 = ps("P1", (128, 512), F32)
    P2 = ps("P2", (128, 512), F32)
    P3 = ps("P3", (128, 512), F32)
    P4 = ps("P4", (128, 512), F32)
    P5 = ps("P5", (128, 512), F32)
    P6 = ps("P6", (128, 512), F32)
    PT = ps("PT", (128, 1024), BF16)

    p = P(nc)
    LD = p.sem("LD")
    LG = p.sem("LG")
    TS = p.sem("TS")
    VS = p.sem("VS")
    SS = p.sem("SS")
    GS = p.sem("GS")

    guard = {}
    def pguard(eng, bank):
        if bank in guard:
            s, v = guard[bank]
            p.wait(eng, s, v)
    def pfree(bank, sem):
        guard[bank] = (sem, p.n(sem))

    def sy(fn):
        p.run("y", fn, inc=LD, k=16)
    def gd(fn):
        p.run("g", fn, inc=LG, k=16)
        p.wait("g", LG, p.n(LG))
    def sy_ser(fn):
        p.run("y", fn, inc=LD, k=16)
        p.wait("y", LD, p.n(LD))

    sy_ser(lambda e: e.dma_start(out=c_id[:, :], in_=ident_h[:, :]))
    sy_ser(lambda e: e.dma_start(out=c_tri[:, :], in_=negtri_h[:, :]))
    sy_ser(lambda e: e.dma_start(out=c_ms[:, :], in_=msk_ms_h[:, :]))
    sy_ser(lambda e: e.dma_start(out=c_dg[:, :], in_=msk_id_h[:, :]))
    sy_ser(lambda e: e.dma_start(out=c_eh[:, :], in_=eh_h[:, :]))
    sy_ser(lambda e: e.dma_start(out=c_e2t[:, :], in_=e2t_h[:, :]))
    sy_ser(lambda e: e.dma_start(out=b1f[:, :], in_=b1f_h[:, :]))
    sy_ser(lambda e: e.dma_start(out=rbb[:, :], in_=rbrep_h[:, :]))
    sy_ser(lambda e: e.dma_start(out=uft[:, :], in_=ufeat_h[:, :]))
    sy_ser(lambda e: e.dma_start(out=maax[:, :], in_=maax_h[:, :]))
    sy_ser(lambda e: e.dma_start(out=lnw[:, :], in_=lnw_h[:, :]))
    sy_ser(lambda e: e.dma_start(out=lnb[:, :], in_=lnb_h[:, :]))
    sy_ser(lambda e: e.dma_start(out=ffg[:, :], in_=fg_h[:, :]))
    sy_ser(lambda e: e.dma_start(out=ffb[:, :], in_=fb_h[:, :]))
    sy(lambda e: e.dma_start(out=x_tok[:].rearrange("p (a c) -> p a c", a=NTC),
                             in_=x_h[0:TT, :].rearrange("(a p) c -> p a c", p=128)))
    LD_setup = p.n(LD)

    gd(lambda e: e.dma_start(out=maa1[:].rearrange("p (k m) -> p k m", k=NJC),
                             in_=maa_w1_h[:].rearrange("(k p) m -> p k m", p=128)))
    gd(lambda e: e.dma_start(out=w2au[:].rearrange("d (s c) -> d s c", s=5),
                             in_=w2aug_h[:, :, :]))
    gd(lambda e: e.dma_start(out=dw1b[:].rearrange("p (k m) -> p k m", k=NJC),
                             in_=dw1_h[:].rearrange("(k p) m -> p k m", p=128)))
    gd(lambda e: e.dma_start(out=dw2b[:, :], in_=dw2aug_h[:, :]))
    gd(lambda e: e.dma_start(out=rwb[:].rearrange("p (k m) -> p k m", k=NJC),
                             in_=rw_h[:].rearrange("(k p) m -> p k m", p=128)))
    gd(lambda e: e.dma_start(out=b2b[:, :], in_=b2rep_h[:, :]))
    LG_small = p.n(LG)

    conv_done = {}
    for nm, src, dst in [("wk", w_k, s_wk), ("wv", w_v, s_wv), ("wr", w_r, s_wr),
                         ("wg", w_g, s_wg), ("wo", w_o, s_wo)]:
        gd(lambda e, src=src, dst=dst: e.dma_start(out=dst[:, :], in_=src[:, :]))
        conv_done[nm] = p.n(LG)
    for ei in range(E):
        gd(lambda e, ei=ei: e.dma_start(out=s_w1[ei, :, :], in_=w1_h[ei, :, :]))
        conv_done[f"w1_{ei}"] = p.n(LG)
    for ei in range(E):
        gd(lambda e, ei=ei: e.dma_start(out=s_w2[ei, :, :], in_=w2_h[ei, :, :]))
        conv_done[f"w2_{ei}"] = p.n(LG)

    def vrun(fn, inc=VS):
        p.run("v", fn, inc=inc)
    def vdr(inc=None):
        p.run("v", lambda e: e.drain(), inc=inc)
    def sdr(inc=None):
        p.run("s", lambda e: e.drain(), inc=inc)
    vrun(lambda e: e.memset(s_st[:, :], 0.0))
    vrun(lambda e: e.memset(xfeat[:].rearrange("p (j q) -> p j q", j=NJC)[:, :, 0:1], 0.0))
    vrun(lambda e: e.memset(tanhb[64:65, :], 1.0))
    for si in range(5):
        vrun(lambda e, si=si: e.memset(mixb[32:33, si * TT:(si + 1) * TT], 1.0))
    vrun(lambda e: e.memset(wgt_pad[:, :], 0.0))
    VS_setup = p.n(VS)

    wa_free = [ (VS, 0), (VS, 0) ]
    wb_free = (VS, 0)
    wc_free = (VS, 0)
    wa_view = lambda slot: wa[:, slot * NJC * 512:(slot + 1) * NJC * 512].rearrange(
        "p (k m) -> p k m", k=NJC)
    wb_view = wb[:].rearrange("p (k m) -> p k m", k=NJC)
    wc_view = wc[:].rearrange("p (k m) -> p k m", k=NJC)
    w2b_view = wb[:].rearrange("p (k m) -> p k m", k=NKH)
    w2c_view = wc[:].rearrange("p (k m) -> p k m", k=NKH)

    def stream_ws_half(src_scr, half, slot, conv_key):
        s_, v_ = wa_free[slot]
        p.wait("y", s_, v_)
        p.wait("y", LG, conv_done[conv_key])
        sy(lambda e: e.dma_start(
            out=wa_view(slot)[:, :, :],
            in_=src_scr.rearrange("(k p) m -> p k m", p=128)[:, :, half * 512:(half + 1) * 512]))
        return p.n(LD)

    def stream_as_full(src_scr, conv_key, which):
        if which == "wb":
            s_, v_ = wb_free
        else:
            s_, v_ = wc_free
        p.wait("y", s_, v_)
        p.wait("y", LG, conv_done[conv_key])
        dst = wb_view if which == "wb" else wc_view
        sy(lambda e: e.dma_start(
            out=dst[:, :, :],
            in_=src_scr.rearrange("(k p) m -> p k m", p=128)[:, :, :]))
        return p.n(LD)

    xfeat3 = xfeat[:].rearrange("p (j q) -> p j q", j=NJC)
    xcur = xfeat3[:, :, 1:TT + 1]
    xsh = xfeat3[:, :, 0:TT]
    xx3 = xx_v.rearrange("p (j q) -> p j q", j=NJC)
    xs3 = xs_v.rearrange("p (j q) -> p j q", j=NJC)
    xt3 = x_tok[:].rearrange("p (a c) -> p a c", a=NTC)
    r3 = r_v.rearrange("p (j q) -> p j q", j=NJC)
    k3 = k_v.rearrange("p (j q) -> p j q", j=NJC)
    v3 = v_v.rearrange("p (a c) -> p a c", a=NTC)
    ew3 = ew_v.rearrange("p (a c) -> p a c", a=NTC)
    g3 = g_sb[:].rearrange("p (j q) -> p j q", j=NJC)
    y3 = ybf[:].rearrange("p (j q) -> p j q", j=NJC)
    yng3 = g3
    z3 = z_v[:].rearrange("p (j q) -> p j q", j=NJC)
    x13 = x1_tok[:].rearrange("p (a c) -> p a c", a=NTC)
    x1z3 = x1z_v.rearrange("p (a c) -> p a c", a=NTC)
    mixed3 = mixed_v[:].rearrange("p (a c) -> p a c", a=NTC)
    h3 = h_v[:].rearrange("p (k q) -> p k q", k=NKH)
    e1c = e1b[:].rearrange("p (j q) -> p j q", j=NJC)
    e2c = e2b[:].rearrange("p (j q) -> p j q", j=NJC)
    rtc = rt_b[:].rearrange("p (j q) -> p j q", j=NJC)
    ktc = kt_b[:].rearrange("p (j q) -> p j q", j=NJC)
    ruc = ru_b[:].rearrange("p (j q) -> p j q", j=NJC)
    ktok3 = ktok[:].rearrange("p (j q) -> p j q", j=NJC)
    sst3 = s_st[:].rearrange("p (j q) -> p j q", j=NJC)
    sbf3 = s_bf[:].rearrange("p (j q) -> p j q", j=NJC)
    lg3 = lg_b[:].rearrange("p (a q) -> p a q", a=NTC)
    rt13 = r_t1[:].rearrange("p (a q) -> p a q", a=NTC)
    wgtp3 = wgt_pad[:].rearrange("p (a q) -> p a q", a=NTC)
    mix_s = lambda si: mixb[0:33, si * TT:(si + 1) * TT]
    mixf32 = mixb[:].bitcast(F32)
    PT2 = PT[:].rearrange("p (a q) -> p a q", a=8)

    x_loaded = p.n(LD)
    xbfs_free = {}

    def dbg_dump(name, out_ap, src_ap):
        p.wait("g", VS, p.n(VS))
        p.wait("g", SS, p.n(SS))
        gd(lambda e, out_ap=out_ap, src_ap=src_ap: e.dma_start(out=out_ap, in_=src_ap))
        p.wait("t", LG, p.n(LG))
        p.wait("v", LG, p.n(LG))
        p.wait("s", LG, p.n(LG))

    def emit_phase_a(t, x_loaded_val, min_ts=0, min_gs=0):
        if t > 0:
            p.wait("v", GS, min_gs)
            vrun(lambda e: e.tensor_copy(xfeat3[:, :, 0:1], xfeat3[:, :, TT:TT + 1]))
        p.wait("s", LD, x_loaded_val)
        p.wait("s", TS, min_ts)
        for tc in range(NTC):
            slot = tc % 2
            xbq = xbfs_v[:, slot * 1024:(slot + 1) * 1024]
            sfree = xbfs_free.get(slot)
            if sfree is not None:
                p.wait("s", sfree[0], sfree[1])
            p.run("s", lambda e, xbq=xbq, tc=tc: e.activation(
                xbq[:, :], xt3[:, tc, :], AF.Copy), inc=SS)
            p.wait("t", SS, p.n(SS))
            pguard("t", "PT")
            for jc in range(NJC):
                last = jc == NJC - 1
                p.trun(0, lambda e, xbq=xbq, jc=jc: e.transpose(
                    PT[:, jc * 128:(jc + 1) * 128], xbq[:, jc * 128:(jc + 1) * 128],
                    c_id[:, :]), inc=TS if last else None)
            xbfs_free[slot] = (TS, p.n(TS))
            p.wait("v", TS, p.n(TS))
            vrun(lambda e, tc=tc: e.tensor_copy(
                xfeat3[:, :, 1 + tc * 128:1 + (tc + 1) * 128],
                PT2[:, :, 0:128]))
            pfree("PT", VS)
        return p.n(VS)

    va_ready = None
    for t in range(NT):
        if t == 0:
            va_ready = emit_phase_a(0, x_loaded)
        elif not HOIST:
            p.wait("y", VS, VS_ln1_prev)
            sy(lambda e, t=t: e.dma_start(
                out=x_tok[:].rearrange("p (a c) -> p a c", a=NTC),
                in_=x_h[t * TT:(t + 1) * TT, :].rearrange("(a p) c -> p a c", p=128)))
            va_ready = emit_phase_a(t, p.n(LD), min_ts=ts_after_H, min_gs=p.n(GS))
        p.wait("v", VS, va_ready)

        vrun(lambda e: e.tensor_tensor(xx3[:, :, :], xsh[:, :, :], xcur[:, :, :], ALU.subtract))
        for jc in range(NJC):
            vrun(lambda e, jc=jc: e.scalar_tensor_tensor(
                xs3[:, jc, :], xx3[:, jc, :], maax[:, jc:jc + 1], xcur[:, jc, :],
                ALU.mult, ALU.add))
        VS_xxx = p.n(VS)

        p.wait("t", VS, VS_xxx)
        p.wait("t", LG, LG_small)
        pguard("t", "P0"); pguard("t", "P1")
        for kc in range(NJC):
            p.trun(0, lambda e, kc=kc: e.matmul(
                P0[:, :], maa1[:, kc * 160:kc * 160 + 128], xs3[:, kc, :],
                start=(kc == 0), stop=(kc == NJC - 1)),
                inc=TS if kc == NJC - 1 else None)
        for kc in range(NJC):
            p.trun(0, lambda e, kc=kc: e.matmul(
                P1[0:32, :], maa1[:, kc * 160 + 128:kc * 160 + 160], xs3[:, kc, :],
                start=(kc == 0), stop=(kc == NJC - 1)),
                inc=TS if kc == NJC - 1 else None)
        p.wait("s", TS, p.n(TS))
        for si in range(4):
            p.run("s", lambda e, si=si: e.activation(
                mix_s(si)[0:32, :], P0[si * 32:(si + 1) * 32, :], AF.Tanh), inc=SS)
        p.run("s", lambda e: e.activation(mix_s(4)[0:32, :], P1[0:32, :], AF.Tanh), inc=SS)
        SS_mix = p.n(SS)
        pfree("P0", SS); pfree("P1", SS)


        def emit_mstream(si):
            p.wait("t", SS, SS_mix)
            evac_marks = []
            for mc in range(NJC):
                bank = (P2, "P2") if mc % 2 == 0 else (P3, "P3")
                pguard("t", bank[1])
                p.trun(0, lambda e, si=si, mc=mc, b=bank[0]: e.matmul(
                    b[:, :], w2au[0:33, si * C + mc * 128:si * C + (mc + 1) * 128],
                    mix_s(si)[0:33, :], start=True, stop=True), inc=TS)
                tsv = p.n(TS)
                p.wait("v", TS, tsv)
                vrun(lambda e, mc=mc, b=bank[0]: e.tensor_tensor(
                    xs3[:, mc, :], xx3[:, mc, :], b[:, :], ALU.mult))
                pfree(bank[1], VS)
                vsv = p.n(VS)
                p.wait("g", VS, vsv)
                p.run("g", lambda e, mc=mc: e.tensor_tensor(
                    xs3[:, mc, :], xs3[:, mc, :], xcur[:, mc, :], ALU.add), inc=GS)
                evac_marks.append(p.n(GS))
            return evac_marks

        gs_w = emit_mstream(0)
        pguard("t", "P4")
        for kc in range(NJC):
            p.wait("t", GS, gs_w[kc])
            p.trun(0, lambda e, kc=kc: e.matmul(
                P4[0:64, :], dw1b[:, kc * HS:(kc + 1) * HS], xs3[:, kc, :],
                start=(kc == 0), stop=(kc == NJC - 1)), inc=TS if kc == NJC - 1 else None)
        p.wait("s", TS, p.n(TS))
        p.run("s", lambda e: e.activation(tanhb[0:64, :], P4[0:64, :], AF.Tanh), inc=SS)
        pfree("P4", SS)
        p.wait("t", SS, p.n(SS))
        for tc in range(NTC):
            for nh in range(2):
                bank = (P5, "P5") if nh == 0 else (P6, "P6")
                pguard("t", bank[1])
                p.trun(0, lambda e, tc=tc, nh=nh, b=bank[0]: e.matmul(
                    b[:, :], tanhb[0:65, tc * 128:(tc + 1) * 128],
                    dw2b[0:65, nh * 512:(nh + 1) * 512], start=True, stop=True), inc=TS)
                p.wait("s", TS, p.n(TS))
                p.run("s", lambda e, tc=tc, nh=nh, b=bank[0]: e.activation(
                    ew3[:, tc, nh * 512:(nh + 1) * 512], b[:, :], AF.Exp), inc=SS)
                pfree(bank[1], SS)
        SS_ew = p.n(SS)

        ld_k0 = stream_ws_half(s_wk, 0, 0, "wk")
        ld_k1 = stream_ws_half(s_wk, 1, 1, "wk")
        gs_k = emit_mstream(1)
        def ws_proj(dst3, ldh, gsm, act=AF.Copy):
            for mc in range(NJC):
                slot = mc // 4
                p.wait("t", LD, ldh[slot])
                bank = (P0, "P0") if mc % 2 == 0 else (P1, "P1")
                pguard("t", bank[1])
                for kc in range(NJC):
                    p.wait("t", GS, gsm[kc])
                    p.trun(0, lambda e, mc=mc, kc=kc, slot=slot, b=bank[0]: e.matmul(
                        b[:, :], wa_view(slot)[:, kc, (mc % 4) * 128:(mc % 4 + 1) * 128],
                        xs3[:, kc, :], start=(kc == 0), stop=(kc == NJC - 1)),
                        inc=TS if kc == NJC - 1 else None)
                p.wait("s", TS, p.n(TS))
                p.run("s", lambda e, mc=mc, b=bank[0]: e.activation(
                    dst3[:, mc, :], b[:, :], act), inc=SS)
                pfree(bank[1], SS)
                if mc == 3:
                    wa_free[0] = (SS, p.n(SS))
                if mc == 7:
                    wa_free[1] = (SS, p.n(SS))
        ws_proj(k3, [ld_k0, ld_k1], gs_k)

        ld_v = stream_as_full(s_wv, "wv", "wb")
        gs_v = emit_mstream(2)
        p.wait("t", LD, ld_v)
        for tc in range(NTC):
            for nh in range(2):
                bank = (P5, "P5") if nh == 0 else (P6, "P6")
                pguard("t", bank[1])
                for kc in range(NJC):
                    p.wait("t", GS, gs_v[kc])
                    p.trun(0, lambda e, tc=tc, nh=nh, kc=kc, b=bank[0]: e.matmul(
                        b[:, :], xs3[:, kc, tc * 128:(tc + 1) * 128],
                        wb_view[:, kc, nh * 512:(nh + 1) * 512],
                        start=(kc == 0), stop=(kc == NJC - 1)),
                        inc=TS if kc == NJC - 1 else None)
                p.wait("s", TS, p.n(TS))
                p.run("s", lambda e, tc=tc, nh=nh, b=bank[0]: e.activation(
                    v3[:, tc, nh * 512:(nh + 1) * 512], b[:, :], AF.Copy), inc=SS)
                pfree(bank[1], SS)
        wb_free = (SS, p.n(SS))

        ld_r0 = stream_ws_half(s_wr, 0, 0, "wr")
        ld_r1 = stream_ws_half(s_wr, 1, 1, "wr")
        gs_r = emit_mstream(3)
        ws_proj(r3, [ld_r0, ld_r1], gs_r)

        ld_g0 = stream_ws_half(s_wg, 0, 0, "wg")
        ld_g1 = stream_ws_half(s_wg, 1, 1, "wg")
        gs_g = emit_mstream(4)
        ws_proj(g3, [ld_g0, ld_g1], gs_g, act=AF.Silu)
        SS_proj = p.n(SS)

        if DBG and t == 0:
            dbg_dump("d_r", dbg["d_r"].rearrange("(j p) q -> p j q", p=128), r3[:, :, :])
            dbg_dump("d_k", dbg["d_k"].rearrange("(j p) q -> p j q", p=128), k3[:, :, :])
            dbg_dump("d_v", dbg["d_v"].rearrange("(a p) c -> p a c", p=128), v3[:, :, :])
            dbg_dump("d_ew", dbg["d_ew"].rearrange("(a p) c -> p a c", p=128), ew3[:, :, :])

        ld_o = stream_as_full(s_wo, "wo", "wb")

        p.wait("v", SS, SS_proj)
        for c in range(NCH):
            tc = (c * L) // 128
            off = (c % 2) * 64
            ccol = c * L

            vrun(lambda e: e.tensor_copy(s_bf[:, :], s_st[:, :]))
            VS_sbf = p.n(VS)

            pguard("t", "P0"); pguard("t", "P1")
            p.wait("t", SS, SS_ew)
            for jc in range(NJC):
                b = P0 if jc < 4 else P1
                p.trun(off, lambda e, jc=jc, b=b, tc=tc, off=off: e.matmul(
                    b[:, (jc % 4) * 128:(jc % 4 + 1) * 128],
                    ew3[off:off + 64, tc, jc * 128:(jc + 1) * 128],
                    c_tri[off:off + 64, :], start=True, stop=True),
                    inc=TS if jc == NJC - 1 else None)
            TS_cum = p.n(TS)

            p.wait("s", TS, TS_cum)
            P04 = P0[:].rearrange("p (a q) -> p a q", a=4)
            P14 = P1[:].rearrange("p (a q) -> p a q", a=4)
            p.run("s", lambda e: e.activation(e1c[:, 0:4, :], P04[:, :, 0:64], AF.Exp), inc=SS)
            p.run("s", lambda e: e.activation(e1c[:, 4:8, :], P14[:, :, 0:64], AF.Exp), inc=SS)
            p.run("s", lambda e: e.activation(e2c[:, 0:4, :], P04[:, :, 64:128], AF.Exp, scale=-1.0), inc=SS)
            p.run("s", lambda e: e.activation(e2c[:, 4:8, :], P14[:, :, 64:128], AF.Exp, scale=-1.0), inc=SS)
            db3 = d_b[:].rearrange("p (a q) -> p a q", a=NJC)
            p.run("s", lambda e: e.activation(db3[:, 0:4, :], P04[:, :, 127:128], AF.Exp), inc=None)
            p.run("s", lambda e: e.activation(db3[:, 4:8, :], P14[:, :, 127:128], AF.Exp), inc=None)
            sdr(inc=SS)
            SS_e = p.n(SS)
            pfree("P0", SS); pfree("P1", SS)

            p.wait("v", SS, SS_e)
            vrun(lambda e, ccol=ccol: e.tensor_tensor(
                rtc[:, :, :], r3[:, :, ccol:ccol + L], e1c[:, :, :], ALU.mult))
            vrun(lambda e, ccol=ccol: e.tensor_tensor(
                ktc[:, :, :], k3[:, :, ccol:ccol + L], e2c[:, :, :], ALU.mult))
            vrun(lambda e, ccol=ccol: e.tensor_tensor(
                ruc[:, :, :], r3[:, :, ccol:ccol + L],
                uft[:, :].unsqueeze(2).broadcast_to([128, NJC, L]),
                ALU.mult))
            VS_rk = p.n(VS)

            p.wait("t", VS, VS_rk)
            pguard("t", "PT")
            for jc in range(NJC):
                p.trun(0, lambda e, jc=jc, off=off: e.transpose(
                    PT[off:off + 64, jc * 128:(jc + 1) * 128],
                    ktc[:, jc, :], c_id[:, :]), inc=TS if jc == NJC - 1 else None)
            p.wait("v", TS, p.n(TS))
            vrun(lambda e, off=off: e.tensor_copy(
                ktok3[off:off + 64, :, :],
                PT2[off:off + 64, :, :]))
            pfree("PT", VS)

            pguard("t", "P2"); pguard("t", "P3"); pguard("t", "P4"); pguard("t", "P5")
            horder = [h for h in range(NH) if h % 2 == 0] + [h for h in range(NH) if h % 2 == 1]
            for hi, h in enumerate(horder):
                hb = (h % 2) * 64
                jc = h // 2
                bA = P2 if h % 2 == 0 else P3
                bB = P4 if h % 2 == 0 else P5
                col = jc * 64
                p.trun(hb, lambda e, hb=hb, jc=jc, bA=bA, col=col, off=off: e.matmul(
                    bA[off:off + 64, col:col + 64],
                    ktc[hb:hb + 64, jc, :], rtc[hb:hb + 64, jc, :],
                    start=True, stop=True), inc=None)
                p.trun(hb, lambda e, hb=hb, jc=jc, bB=bB, col=col, off=off, ccol=ccol: e.matmul(
                    bB[off:off + 64, col:col + 64],
                    k3[hb:hb + 64, jc, ccol:ccol + L], ruc[hb:hb + 64, jc, :],
                    start=True, stop=True), inc=(TS if hi == NH - 1 else None))
            TS_A = p.n(TS)

            p.wait("v", TS, TS_A)
            a_sb3 = a_sb[:].rearrange("p (h q) -> p h q", h=NH)
            a2m3 = a2m[:].rearrange("p (h q) -> p h q", h=NH)
            P2r = P2[:].rearrange("p (a q) -> p a q", a=NJC)
            P3r = P3[:].rearrange("p (a q) -> p a q", a=NJC)
            P4r = P4[:].rearrange("p (a q) -> p a q", a=NJC)
            P5r = P5[:].rearrange("p (a q) -> p a q", a=NJC)
            cms3 = c_ms[:].rearrange("p (h q) -> p h q", h=NH)
            cdg3 = c_dg[:].rearrange("p (h q) -> p h q", h=NH)
            vrun(lambda e, off=off: e.tensor_tensor(
                a_sb3[off:off + 64, 0:NH:2, :], P2r[off:off + 64, :, :],
                cms3[off:off + 64, 0:NH:2, :], ALU.mult))
            vrun(lambda e, off=off: e.tensor_tensor(
                a_sb3[off:off + 64, 1:NH:2, :], P3r[off:off + 64, :, :],
                cms3[off:off + 64, 1:NH:2, :], ALU.mult))
            vrun(lambda e, off=off: e.tensor_tensor(
                a2m3[off:off + 64, 0:NH:2, :], P4r[off:off + 64, :, :],
                cdg3[off:off + 64, 0:NH:2, :], ALU.mult))
            vrun(lambda e, off=off: e.tensor_tensor(
                a2m3[off:off + 64, 1:NH:2, :], P5r[off:off + 64, :, :],
                cdg3[off:off + 64, 1:NH:2, :], ALU.mult))
            vrun(lambda e, off=off: e.tensor_tensor(
                a_sb[off:off + 64, :], a_sb[off:off + 64, :], a2m[off:off + 64, :], ALU.add))
            VS_A = p.n(VS)
            pfree("P2", VS); pfree("P3", VS); pfree("P4", VS); pfree("P5", VS)

            pguard("t", "P6"); pguard("t", "P2"); pguard("t", "P3")
            p.wait("t", VS, VS_A)
            for h in range(NH):
                hb = (h % 2) * 64
                jc = h // 2
                col = jc * 64
                p.trun(off, lambda e, h=h, hb=hb, jc=jc, col=col, off=off, tc=tc: e.matmul(
                    P6[hb:hb + 64, col:col + 64],
                    v3[off:off + 64, tc, h * 64:(h + 1) * 64],
                    a_sb[off:off + 64, h * 64:(h + 1) * 64],
                    start=True, stop=True), inc=None)
            for h in range(0, NH, 2):
                jc = h // 2
                p.trun(0, lambda e, jc=jc: e.matmul(
                    P2[0:64, jc * 64:(jc + 1) * 64],
                    sbf3[0:64, jc, :], rtc[0:64, jc, :],
                    start=True, stop=True), inc=None)
            for hi, h in enumerate(range(1, NH, 2)):
                jc = h // 2
                p.trun(64, lambda e, jc=jc: e.matmul(
                    P3[64:128, jc * 64:(jc + 1) * 64],
                    sbf3[64:128, jc, :], rtc[64:128, jc, :],
                    start=True, stop=True), inc=(TS if hi == 7 else None))
            TS_y = p.n(TS)

            p.wait("v", TS, TS_y)
            vrun(lambda e, ccol=ccol: e.tensor_copy(
                y3[:, :, ccol:ccol + L], P6[:].rearrange("p (a q) -> p a q", a=NJC)))
            pfree("P6", VS)
            vrun(lambda e, ccol=ccol: e.tensor_tensor(
                y3[0:64, :, ccol:ccol + L], y3[0:64, :, ccol:ccol + L],
                P2[:].rearrange("p (a q) -> p a q", a=NJC)[0:64, :, :], ALU.add))
            pfree("P2", VS)
            vrun(lambda e, ccol=ccol: e.tensor_tensor(
                y3[64:128, :, ccol:ccol + L], y3[64:128, :, ccol:ccol + L],
                P3[:].rearrange("p (a q) -> p a q", a=NJC)[64:128, :, :], ALU.add))
            pfree("P3", VS)

            pguard("t", "P0"); pguard("t", "P1")
            for pr in range(4):
                p.trun(off, lambda e, pr=pr, off=off, tc=tc: e.matmul(
                    P0[:, pr * 128:(pr + 1) * 128],
                    ktok3[off:off + 64, pr, :],
                    v3[off:off + 64, tc, pr * 128:(pr + 1) * 128],
                    start=True, stop=True), inc=None)
            for pr in range(4, NJC):
                p.trun(off, lambda e, pr=pr, off=off, tc=tc: e.matmul(
                    P1[:, (pr - 4) * 128:(pr - 3) * 128],
                    ktok3[off:off + 64, pr, :],
                    v3[off:off + 64, tc, pr * 128:(pr + 1) * 128],
                    start=True, stop=True), inc=(TS if pr == NJC - 1 else None))
            TS_U = p.n(TS)

            p.wait("v", TS, TS_U)
            P0u = P0[:].rearrange("p (a q) -> p a q", a=4)
            P1u = P1[:].rearrange("p (a q) -> p a q", a=4)
            vrun(lambda e: e.tensor_tensor(
                sst3[0:64, 0:4, :], sst3[0:64, 0:4, :], P0u[0:64, :, 0:64], ALU.add))
            vrun(lambda e: e.tensor_tensor(
                sst3[64:128, 0:4, :], sst3[64:128, 0:4, :], P0u[64:128, :, 64:128], ALU.add))
            vrun(lambda e: e.tensor_tensor(
                sst3[0:64, 4:8, :], sst3[0:64, 4:8, :], P1u[0:64, :, 0:64], ALU.add))
            vrun(lambda e: e.tensor_tensor(
                sst3[64:128, 4:8, :], sst3[64:128, 4:8, :], P1u[64:128, :, 64:128], ALU.add))
            vrun(lambda e: e.tensor_tensor(
                sst3[:, :, :], sst3[:, :, :],
                d_b[:, :].unsqueeze(2).broadcast_to([128, NJC, HS]),
                ALU.mult))
            pfree("P0", VS); pfree("P1", VS)
        VS_wkv = p.n(VS)

        p.wait("s", VS, VS_wkv)
        p.run("s", lambda e: e.activation(y2_v[:, :], ybf[:, :], AF.Square), inc=SS)
        SS_y2 = p.n(SS)
        y23 = y2_v.rearrange("p (j q) -> p j q", j=NJC)
        p.wait("t", VS, VS_wkv)
        p.wait("t", SS, SS_y2)
        stat_banks = [(P2, "P2"), (P3, "P3"), (P4, "P4"), (P5, "P5")]
        for jc in range(NJC):
            bank = stat_banks[jc // 2]
            o1 = (jc % 2) * 64
            pguard("t", bank[1])
            tp1 = {"tile_position": (0, o1)} if o1 else {}
            tp2 = {"tile_position": (0, o1 + 32)}
            p.trun(0, lambda e, jc=jc, b=bank[0], o1=o1, tp1=tp1: e.matmul(
                b[o1:o1 + 2, :], c_eh[:, :], y3[:, jc, :], start=True, stop=True, **tp1),
                inc=None)
            p.trun(0, lambda e, jc=jc, b=bank[0], o1=o1, tp2=tp2, y23=y23: e.matmul(
                b[o1 + 32:o1 + 34, :], c_eh[:, :], y23[:, jc, :], start=True, stop=True, **tp2),
                inc=TS)
        mu_slots = [r_g1[0:2, :], r_g2[0:2, :]]
        varc_slots = [r_g3[0:2, :], mixf32[0:2, 0:TT]]
        tmp2 = sq_v[0:2, 0:TT]
        gab4 = gab[:].rearrange("p (a q) -> p a q", a=4)
        bc_banks_a = [(P6, "P6"), (P1, "P1")]
        for jc in range(NJC):
            bank = stat_banks[jc // 2]
            o1 = (jc % 2) * 64
            mu = mu_slots[jc % 2]
            varc = varc_slots[jc % 2]
            p.wait("v", TS, p.n(TS) - (NJC - 1 - jc) * 0)
            vrun(lambda e, b=bank[0], o1=o1, mu=mu: e.tensor_scalar(
                mu, b[o1:o1 + 2, :], 1.0 / HS, None, ALU.mult))
            vrun(lambda e, b=bank[0], o1=o1, varc=varc: e.tensor_scalar(
                varc, b[o1 + 32:o1 + 34, :], 1.0 / HS, None, ALU.mult))
            if jc % 2 == 1:
                pfree(bank[1], VS)
            vrun(lambda e, mu=mu: e.tensor_tensor(tmp2, mu, mu, ALU.mult))
            vrun(lambda e, varc=varc: e.tensor_tensor(varc, varc, tmp2, ALU.subtract))
            vrun(lambda e, varc=varc: e.tensor_scalar(
                varc, varc, 0.0, GN_EPS, ALU.max, ALU.add))
            VS_varc = p.n(VS)
            p.wait("s", VS, VS_varc)
            p.run("s", lambda e, varc=varc: e.activation(varc, varc, AF.Ln), inc=None)
            p.run("s", lambda e, varc=varc, jc=jc, gab4=gab4: e.activation(
                gab4[0:2, 2 * (jc % 2), :], varc, AF.Exp, scale=-0.5), inc=None)
            sdr(inc=SS)
            p.wait("v", SS, p.n(SS))
            vrun(lambda e, mu=mu, jc=jc, gab4=gab4: e.scalar_tensor_tensor(
                gab4[0:2, 2 * (jc % 2) + 1, :], mu, -1.0, gab4[0:2, 2 * (jc % 2), :],
                ALU.mult, ALU.mult))
            VS_ab = p.n(VS)
            bcA = bc_banks_a[jc % 2]
            if jc == 0:
                bcB = (P0, "P0")
            else:
                bcB = stat_banks[(jc - 1) // 2] if jc % 2 == 1 else (P0, "P0")
            p.wait("t", VS, VS_ab)
            pguard("t", bcA[1]); pguard("t", bcB[1])
            p.trun(0, lambda e, jc=jc, b=bcA[0], gab4=gab4: e.matmul(
                b[:, :], c_e2t[0:2, :], gab4[0:2, 2 * (jc % 2), :], start=True, stop=True),
                inc=None)
            p.trun(0, lambda e, jc=jc, b=bcB[0], gab4=gab4: e.matmul(
                b[:, :], c_e2t[0:2, :], gab4[0:2, 2 * (jc % 2) + 1, :], start=True, stop=True),
                inc=TS)
            p.wait("v", TS, p.n(TS))
            vrun(lambda e, jc=jc, b=bcA[0]: e.tensor_tensor(
                a2m[:, 0:TT], y3[:, jc, :], b[:, :], ALU.mult))
            pfree(bcA[1], VS)
            vrun(lambda e, jc=jc, b=bcB[0]: e.tensor_tensor(
                a2m[:, 0:TT], a2m[:, 0:TT], b[:, :], ALU.add))
            pfree(bcB[1], VS)
            vrun(lambda e, jc=jc: e.tensor_scalar(
                a2m[:, 0:TT], a2m[:, 0:TT], lnw[:, jc:jc + 1], lnb[:, jc:jc + 1],
                ALU.mult, ALU.add))
            vrun(lambda e, jc=jc: e.tensor_tensor(
                yng3[:, jc, :], a2m[:, 0:TT], g3[:, jc, :], ALU.mult))
        VS_yng = p.n(VS)

        if DBG and t == 0:
            dbg_dump("d_y", dbg["d_y"].rearrange("(j p) q -> p j q", p=128), y3[:, :, :])
            dbg_dump("d_yng", dbg["d_yng"].rearrange("(j p) q -> p j q", p=128), yng3[:, :, :])

        p.wait("t", VS, VS_yng)
        p.wait("t", LD, ld_o)
        for tc in range(NTC):
            for nh in range(2):
                bank = (P5, "P5") if nh == 0 else (P6, "P6")
                pguard("t", bank[1])
                for kc in range(NJC):
                    p.trun(0, lambda e, tc=tc, nh=nh, kc=kc, b=bank[0]: e.matmul(
                        b[:, :], yng3[:, kc, tc * 128:(tc + 1) * 128],
                        wb_view[:, kc, nh * 512:(nh + 1) * 512],
                        start=(kc == 0), stop=(kc == NJC - 1)),
                        inc=TS if kc == NJC - 1 else None)
                p.wait("v", TS, p.n(TS))
                vrun(lambda e, tc=tc, nh=nh, b=bank[0]: e.tensor_tensor(
                    xt3[:, tc, nh * 512:(nh + 1) * 512],
                    xt3[:, tc, nh * 512:(nh + 1) * 512], b[:, :], ALU.add))
                pfree(bank[1], VS)
        wb_free = (VS, p.n(VS))
        for tc in range(NTC):
            vrun(lambda e, tc=tc: e.reduce_sum(
                ln_a[:, tc:tc + 1], xt3[:, tc, :], mybir.AxisListType.X))
            p.wait("s", VS, p.n(VS))
            sqs = sq_v if tc % 2 == 0 else sq2_v
            p.run("s", lambda e, tc=tc, sqs=sqs: e.activation(
                sqs[:, :], xt3[:, tc, :], AF.Square), inc=SS)
            p.wait("v", SS, p.n(SS))
            vrun(lambda e, tc=tc, sqs=sqs: e.reduce_sum(
                ln_b2[:, tc:tc + 1], sqs[:, :], mybir.AxisListType.X))
        vdr()
        vrun(lambda e: e.tensor_scalar(ln_a[:, :], ln_a[:, :], 1.0 / C, None, ALU.mult))
        vdr()
        vrun(lambda e: e.tensor_tensor(ln_c[:, :], ln_a[:, :], ln_a[:, :], ALU.mult))
        vrun(lambda e: e.tensor_scalar(ln_d[:, :], ln_b2[:, :], 1.0 / C, None, ALU.mult))
        vdr()
        vrun(lambda e: e.tensor_tensor(ln_d[:, :], ln_d[:, :], ln_c[:, :], ALU.subtract))
        vdr()
        vrun(lambda e: e.tensor_scalar(ln_c[:, :], ln_d[:, :], LN_EPS, None, ALU.add))
        vrun(lambda e: e.tensor_scalar(ln_b2[:, :], ln_d[:, :], LN_EPS, None, ALU.add))
        vdr(inc=VS)
        p.wait("s", VS, p.n(VS))
        p.run("s", lambda e: e.activation(ln_c[:, :], ln_c[:, :], AF.Sqrt), inc=None)
        sdr(inc=SS)
        p.wait("v", SS, p.n(SS))
        vrun(lambda e: e.reciprocal(ln_c[:, :], ln_c[:, :]))
        vrun(lambda e: e.reciprocal(ln_b2[:, :], ln_b2[:, :]))
        vdr()
        vrun(lambda e: e.tensor_tensor(ln_b2[:, :], ln_d[:, :], ln_b2[:, :], ALU.mult))
        vdr()
        vrun(lambda e: e.tensor_scalar(ln_b2[:, :], ln_b2[:, :], LN_EPS, None, ALU.add))
        vdr(inc=VS)
        p.wait("s", VS, p.n(VS))
        p.run("s", lambda e: e.activation(ln_b2[:, :], ln_b2[:, :], AF.Sqrt), inc=None)
        sdr(inc=SS)
        p.wait("v", SS, p.n(SS))
        vrun(lambda e: e.reciprocal(ln_b2[:, :], ln_b2[:, :]))
        vdr()
        for tc in range(NTC):
            vrun(lambda e, tc=tc: e.tensor_scalar(
                x13[:, tc, :], xt3[:, tc, :], ln_a[:, tc:tc + 1], ln_c[:, tc:tc + 1],
                ALU.subtract, ALU.mult))
            vrun(lambda e, tc=tc: e.tensor_scalar(
                x1z3[:, tc, :], x13[:, tc, :], ln_b2[:, tc:tc + 1], None, ALU.mult))
        VS_ln1 = p.n(VS)
        VS_ln1_prev = VS_ln1

        p.wait("t", VS, VS_ln1)
        for tc in range(NTC):
            pguard("t", "PT")
            for jc in range(NJC):
                p.trun(0, lambda e, tc=tc, jc=jc: e.transpose(
                    PT[:, jc * 128:(jc + 1) * 128],
                    x1z3[:, tc, jc * 128:(jc + 1) * 128], c_id[:, :]),
                    inc=TS if jc == NJC - 1 else None)
            p.wait("v", TS, p.n(TS))
            vrun(lambda e, tc=tc: e.tensor_copy(
                z3[:, :, tc * 128:(tc + 1) * 128], PT2[:, :, 0:128]))
            pfree("PT", VS)
        ts_after_H = p.n(TS)
        for jc in range(NJC):
            vrun(lambda e, jc=jc: e.tensor_scalar(
                z3[:, jc, :], z3[:, jc, :], ffg[:, jc:jc + 1], ffb[:, jc:jc + 1],
                ALU.mult, ALU.add))
        VS_z = p.n(VS)

        if DBG and t == 0:
            dbg_dump("d_x1", dbg["d_x1"].rearrange("(a p) c -> p a c", p=128), x13[:, :, :])
            dbg_dump("d_x1z", dbg["d_x1z"].rearrange("(a p) c -> p a c", p=128), x1z3[:, :, :])
            dbg_dump("d_z", dbg["d_z"].rearrange("(j p) q -> p j q", p=128), z3[:, :, :])

        p.wait("t", VS, VS_z)
        pguard("t", "P2")
        for tc in range(NTC):
            for kc in range(NJC):
                p.trun(0, lambda e, tc=tc, kc=kc: e.matmul(
                    P2[:, tc * E:(tc + 1) * E], z3[:, kc, tc * 128:(tc + 1) * 128],
                    rwb[:, kc * E:(kc + 1) * E], start=(kc == 0), stop=(kc == NJC - 1)),
                    inc=TS if kc == NJC - 1 else None)
        p.wait("v", TS, p.n(TS))
        vrun(lambda e: e.tensor_copy(lg_b[:, :], P2[:, 0:NTC * E]))
        pfree("P2", VS)
        vdr()
        vrun(lambda e: e.tensor_tensor(
            lg3[:, :, :], lg3[:, :, :],
            rbb[:, :].unsqueeze(1).broadcast_to([128, NTC, E]), ALU.add))
        vdr()
        vrun(lambda e: e.reduce_max(r_m1[:, :], lg3[:, :, :], mybir.AxisListType.X))
        vdr()
        vrun(lambda e: e.tensor_tensor(
            rt13[:, :, :], lg3[:, :, :],
            r_m1[:, :].unsqueeze(2).broadcast_to([128, NTC, E]), ALU.is_ge))
        vdr()
        vrun(lambda e: e.scalar_tensor_tensor(
            rt13[:, :, :], rt13[:, :, :], -1e30, lg3[:, :, :], ALU.mult, ALU.add))
        vdr()
        vrun(lambda e: e.reduce_max(r_m2[:, :], rt13[:, :, :], mybir.AxisListType.X))
        vdr()
        vrun(lambda e: e.tensor_tensor(
            rt13[:, :, :], lg3[:, :, :],
            r_m2[:, :].unsqueeze(2).broadcast_to([128, NTC, E]), ALU.is_ge))
        vrun(lambda e: e.tensor_tensor(r_m2[:, :], r_m2[:, :], r_m1[:, :], ALU.subtract))
        vdr()
        vrun(lambda e: e.tensor_tensor(
            lg3[:, :, :], lg3[:, :, :],
            r_m1[:, :].unsqueeze(2).broadcast_to([128, NTC, E]), ALU.subtract))
        vdr(inc=VS)
        p.wait("s", VS, p.n(VS))
        p.run("s", lambda e: e.activation(lg_b[:, :], lg_b[:, :], AF.Exp), inc=None)
        p.run("s", lambda e: e.activation(r_m1[:, :], r_m2[:, :], AF.Sigmoid, scale=-1.0), inc=None)
        sdr(inc=SS)
        SS_r = p.n(SS)
        p.wait("v", SS, SS_r)
        vrun(lambda e: e.tensor_tensor(lg3[:, :, :], lg3[:, :, :], rt13[:, :, :], ALU.mult))
        vdr()
        vrun(lambda e: e.tensor_tensor(
            lg3[:, :, :], lg3[:, :, :],
            r_m1[:, :].unsqueeze(2).broadcast_to([128, NTC, E]), ALU.mult))
        vdr()
        wp3 = wgt_pad[:].rearrange("p (a q) -> p a q", a=NTC)
        vrun(lambda e: e.tensor_copy(wp3[:, :, 0:E], lg3[:, :, :]))
        vdr()
        vrun(lambda e: e.tensor_copy(wgt_bf[:, :], wgt_pad[:, :]))
        vdr()
        VS_wgt = p.n(VS)
        p.wait("t", VS, VS_wgt)
        pguard("t", "PT")
        for tc in range(NTC):
            p.trun(0, lambda e, tc=tc: e.transpose(
                PT[0:32, tc * 128:(tc + 1) * 128], wgt_bf[:, tc * 32:(tc + 1) * 32],
                c_id[:, :]), inc=TS if tc == NTC - 1 else None)
        p.wait("v", TS, p.n(TS))
        vrun(lambda e: e.tensor_copy(wgt_t[0:32, :], PT[0:32, 0:512]))
        pfree("PT", VS)
        VS_router = p.n(VS)

        if DBG and t == 0:
            dbg_dump("d_wgt", dbg["d_wgt"].rearrange("(a p) q -> p a q", p=128), lg3[:, :, :])

        p.wait("v", VS, VS_router)
        vrun(lambda e: e.tensor_scalar(mixed_v[:, :], x1_tok[:, :], 2.0, None, ALU.mult))
        VS_minit = p.n(VS)

        ld_w1 = {}
        ld_w2 = {}
        def stream_w1_quarter(ei, q, slot):
            s_, v_ = wa_free[slot]
            p.wait("y", s_, v_)
            p.wait("y", LG, conv_done[f"w1_{ei}"])
            sy(lambda e, ei=ei, q=q, slot=slot: e.dma_start(
                out=wa_view(slot)[:, :, :],
                in_=s_w1[ei].rearrange("(k p) m -> p k m", p=128)[:, :, q * 512:(q + 1) * 512]))
            return p.n(LD)

        def stream_w2_half(ei, hh, which):
            nonlocal wb_free, wc_free
            s_, v_ = wb_free if which == "wb" else wc_free
            p.wait("y", s_, v_)
            p.wait("y", LG, conv_done[f"w2_{ei}"])
            dst = w2b_view if which == "wb" else w2c_view
            sy(lambda e, ei=ei, hh=hh, dst=dst: e.dma_start(
                out=dst[:, :, :],
                in_=s_w2[ei].rearrange("(k p) m -> p k m", p=128)[:, :, hh * 512:(hh + 1) * 512]))
            return p.n(LD)

        gs_after_g = p.n(GS)
        for ei in range(E):
            if HOIST and ei == 1 and t + 1 < NT:
                p.wait("y", VS, VS_ln1)
                sy(lambda e, t=t: e.dma_start(
                    out=x_tok[:].rearrange("p (a c) -> p a c", a=NTC),
                    in_=x_h[(t + 1) * TT:(t + 2) * TT, :].rearrange("(a p) c -> p a c", p=128)))
                x_loaded = p.n(LD)
                va_ready = emit_phase_a(t + 1, x_loaded, min_ts=ts_after_H, min_gs=gs_after_g)
            for q in range(4):
                ld_w1[(ei, q)] = stream_w1_quarter(ei, q, q % 2)
                p.wait("t", LD, ld_w1[(ei, q)])
                p.wait("t", VS, VS_minit)
                for mi in range(4):
                    mh = q * 4 + mi
                    bank = (P0, "P0") if mh % 2 == 0 else (P1, "P1")
                    pguard("t", bank[1])
                    for kc in range(NJC):
                        p.trun(0, lambda e, q=q, mi=mi, kc=kc, b=bank[0]: e.matmul(
                            b[:, :], wa_view(q % 2)[:, kc, mi * 128:(mi + 1) * 128],
                            z3[:, kc, :], start=(kc == 0), stop=(kc == NJC - 1)),
                            inc=TS if kc == NJC - 1 else None)
                    p.wait("s", TS, p.n(TS))
                    p.run("s", lambda e, ei=ei, mh=mh, b=bank[0]: e.activation(
                        h3[:, mh, :], b[:, :], AF.Gelu,
                        bias=b1f[:, ei * NKH + mh:ei * NKH + mh + 1]), inc=SS)
                    pfree(bank[1], SS)
                wa_free[q % 2] = (SS, p.n(SS))
            SS_h = p.n(SS)
            for hh in range(2):
                which = "wb" if hh == 0 else "wc"
                ld_w2[(ei, hh)] = stream_w2_half(ei, hh, which)
                wv_ = w2b_view if hh == 0 else w2c_view
                p.wait("t", LD, ld_w2[(ei, hh)])
                p.wait("t", SS, SS_h)
                for tc in range(NTC):
                    bank = [(P2, "P2"), (P3, "P3"), (P4, "P4"), (P5, "P5")][tc]
                    pguard("t", bank[1])
                    p.trun(0, lambda e, tc=tc, ei=ei, hh=hh, b=bank[0]: e.matmul(
                        b[:, :], wgt_t[0:8, tc * 128:(tc + 1) * 128],
                        b2b[0:8, hh * 512:(hh + 1) * 512],
                        start=True, stop=False), inc=None)
                    for kk in range(NKH):
                        p.trun(0, lambda e, tc=tc, kk=kk, wv_=wv_, b=bank[0]: e.matmul(
                            b[:, :], h3[:, kk, tc * 128:(tc + 1) * 128],
                            wv_[:, kk, :], start=False, stop=(kk == NKH - 1)),
                            inc=TS if kk == NKH - 1 else None)
                    p.wait("v", TS, p.n(TS))
                    vrun(lambda e, tc=tc, hh=hh, ei=ei, b=bank[0]: e.scalar_tensor_tensor(
                        mixed3[:, tc, hh * 512:(hh + 1) * 512], b[:, :],
                        wgtp3[:, tc, ei:ei + 1],
                        mixed3[:, tc, hh * 512:(hh + 1) * 512], ALU.mult, ALU.add))
                    pfree(bank[1], VS)
                if hh == 0:
                    wb_free = (VS, p.n(VS))
                else:
                    wc_free = (VS, p.n(VS))
        VS_ffn = p.n(VS)

        if DBG and t == 0:
            dbg_dump("d_mixed", dbg["d_mixed"].rearrange("(a p) c -> p a c", p=128), mixed3[:, :, :])

        p.wait("v", VS, VS_ffn)
        for tc in range(NTC):
            vrun(lambda e, tc=tc: e.reduce_sum(
                ln_a[:, tc:tc + 1], mixed3[:, tc, :], mybir.AxisListType.X))
            p.wait("s", VS, p.n(VS) - 1 + 1)
            sqs = sq_v if tc % 2 == 0 else sq2_v
            p.wait("s", VS, VS_ffn)
            p.run("s", lambda e, tc=tc, sqs=sqs: e.activation(
                sqs[:, :], mixed3[:, tc, :], AF.Square), inc=SS)
            p.wait("v", SS, p.n(SS))
            vrun(lambda e, tc=tc, sqs=sqs: e.reduce_sum(
                ln_b2[:, tc:tc + 1], sqs[:, :], mybir.AxisListType.X))
        vdr()
        vrun(lambda e: e.tensor_scalar(ln_a[:, :], ln_a[:, :], 1.0 / C, None, ALU.mult))
        vdr()
        vrun(lambda e: e.tensor_tensor(ln_c[:, :], ln_a[:, :], ln_a[:, :], ALU.mult))
        vrun(lambda e: e.tensor_scalar(ln_d[:, :], ln_b2[:, :], 1.0 / C, None, ALU.mult))
        vdr()
        vrun(lambda e: e.tensor_tensor(ln_d[:, :], ln_d[:, :], ln_c[:, :], ALU.subtract))
        vdr()
        vrun(lambda e: e.tensor_scalar(ln_c[:, :], ln_d[:, :], LN_EPS, None, ALU.add))
        vdr(inc=VS)
        p.wait("s", VS, p.n(VS))
        p.run("s", lambda e: e.activation(ln_c[:, :], ln_c[:, :], AF.Sqrt), inc=None)
        sdr(inc=SS)
        p.wait("v", SS, p.n(SS))
        vrun(lambda e: e.reciprocal(ln_c[:, :], ln_c[:, :]))
        vdr()
        for tc in range(NTC):
            vrun(lambda e, tc=tc: e.tensor_scalar(
                mixed3[:, tc, :], mixed3[:, tc, :], ln_a[:, tc:tc + 1], ln_c[:, tc:tc + 1],
                ALU.subtract, ALU.mult))
        VS_out = p.n(VS)
        p.wait("y", VS, VS_out)
        sy(lambda e, t=t: e.dma_start(
            out=out_h[t * TT:(t + 1) * TT, :].rearrange("(a p) c -> p a c", p=128),
            in_=mixed3[:, :, :]))
        out_done = p.n(LD)



    p.wait("y", LD, p.n(LD))

    p.emit()
    return nc, es


def make_consts():
    ident = np.eye(128, dtype=BF)
    negtri = np.zeros((128, 128), dtype=np.float32)
    for half in range(2):
        r0 = half * 64
        for tau in range(64):
            for tt_ in range(64):
                if tau < tt_:
                    negtri[r0 + tau, tt_] = -1.0
                if tau <= tt_:
                    negtri[r0 + tau, 64 + tt_] = -1.0
    ms = np.zeros((128, NH * L), dtype=np.float32)
    dg = np.zeros((128, NH * L), dtype=np.float32)
    for r in range(128):
        s = r % 64
        for h in range(NH):
            for tt_ in range(L):
                if s < tt_:
                    ms[r, h * L + tt_] = 1.0
                if s == tt_:
                    dg[r, h * L + tt_] = 1.0
    eh = np.zeros((128, 2), dtype=np.float32)
    eh[0:64, 0] = 1.0
    eh[64:128, 1] = 1.0
    e2t = np.zeros((2, 128), dtype=np.float32)
    e2t[0, 0:64] = 1.0
    e2t[1, 64:128] = 1.0
    return dict(c_ident=ident, c_negtri=negtri.astype(BF), c_ms=ms.astype(BF),
                c_id=dg.astype(BF), c_eh=eh.astype(BF), c_e2t=e2t.astype(BF))


def make_in_map(core_x, inp):
    C_ = C
    def pp(vec):
        return np.ascontiguousarray(vec.reshape(NJC, 128).T)
    maa_c = np.stack([inp["L0_attn_time_maa_w"], inp["L0_attn_time_maa_k"],
                      inp["L0_attn_time_maa_v"], inp["L0_attn_time_maa_r"],
                      inp["L0_attn_time_maa_g"]], axis=0)
    w2 = inp["L0_attn_time_maa_w2"]
    w2aug = np.concatenate([w2, maa_c[:, None, :]], axis=1)
    w2aug = np.ascontiguousarray(w2aug.transpose(1, 0, 2))
    dw2aug = np.concatenate([inp["L0_attn_time_decay_w2"],
                             inp["L0_attn_time_decay"][None, :]], axis=0)
    b1f = np.ascontiguousarray(inp["L0_ffn_b1"].reshape(E, NKH, 128).transpose(2, 0, 1)
                               .reshape(128, E * NKH))
    b2rep = np.zeros((128, C_), dtype=np.float32)
    for tcq in range(4):
        b2rep[tcq * 32:tcq * 32 + 8, :] = inp["L0_ffn_b2"]
    rbrep = np.broadcast_to(inp["L0_ffn_rb"][None, :], (128, E)).copy()
    u = inp["L0_attn_time_faaaa"]
    ufeat = np.zeros((128, NJC), dtype=np.float32)
    for pr in range(128):
        for jc in range(NJC):
            ufeat[pr, jc] = u[2 * jc + pr // 64, pr % 64]
    m = dict(
        x=np.ascontiguousarray(core_x),
        w_r=inp["L0_attn_Wr"], w_k=inp["L0_attn_Wk"], w_v=inp["L0_attn_Wv"],
        w_g=inp["L0_attn_Wg"], w_o=inp["L0_attn_Wo"],
        ffn_w1=inp["L0_ffn_W1"], ffn_w2=inp["L0_ffn_W2"],
        maa_w1=inp["L0_attn_time_maa_w1"],
        w2aug=w2aug, dw1=inp["L0_attn_time_decay_w1"], dw2aug=dw2aug,
        rw=inp["L0_ffn_rW"], b1f=b1f, b2rep=b2rep, rbrep=rbrep,
        ufeat=ufeat, maax_pp=pp(inp["L0_attn_time_maa_x"]),
        lnw_pp=pp(inp["L0_attn_ln_w"]), lnb_pp=pp(inp["L0_attn_ln_b"]),
        ffng_pp=pp(inp["L0_ffn_ln_g"]), ffnb_pp=pp(inp["L0_ffn_ln_b"]),
    )
    m.update(make_consts())
    return m


def kernel(**inputs):
    x = inputs["L0_attn_x"]
    B, T, C_ = x.shape
    nc, es = _build_cached(T)
    in_maps = [make_in_map(x[b], inputs) for b in range(B)]
    res = run_bass_kernel_spmd(nc, in_maps, core_ids=list(range(B)))
    out = np.stack([np.asarray(res.results[b]["out"]) for b in range(B)], axis=0)
    return out.astype(np.float32)


_cache = {}

def _build_cached(T):
    if T not in _cache:
        _cache[T] = _do_build(T)
    return _cache[T]


def _do_build(T):
    return build_program(T=T)
